# version 01
# baseline (speedup 1.0000x reference)
import numpy as np
import ml_dtypes
from contextlib import ExitStack

import concourse.bass as bass
import concourse.mybir as mybir
from concourse.bass import IndirectOffsetOnAxis
from concourse.bass_utils import run_bass_kernel_spmd

F32 = mybir.dt.float32
BF16 = mybir.dt.bfloat16
I32 = mybir.dt.int32
U32 = mybir.dt.uint32
AF = mybir.ActivationFunctionType
OP = mybir.AluOpType
AX = mybir.AxisListType

T, D, H, DH, E, CAP, HID = 2048, 1024, 16, 64, 8, 320, 4096
TT, CT, QC = T // 128, D // 128, T // 512
NS = E * CAP
BUFROWS = 2688
DEBUG = False


def build_core():
    nc = bass.Bass("TRN2", target_bir_lowering=False)
    es = ExitStack()

    d_x = nc.dram_tensor("x", [T, D], F32, kind="ExternalInput").ap()
    d_wqkvT = nc.dram_tensor("wqkvT", [D, 3 * D], F32, kind="ExternalInput").ap()
    d_woutT = nc.dram_tensor("woutT", [D, D], F32, kind="ExternalInput").ap()
    d_boutr = nc.dram_tensor("boutrep", [128, D], F32, kind="ExternalInput").ap()
    d_wr = nc.dram_tensor("wrp", [128, 8, 8], F32, kind="ExternalInput").ap()
    d_w1 = nc.dram_tensor("w1", [E, D, HID], F32, kind="ExternalInput").ap()
    d_w2 = nc.dram_tensor("w2", [E, HID, D], F32, kind="ExternalInput").ap()
    d_b1t = nc.dram_tensor("b1t", [128, E, 32], F32, kind="ExternalInput").ap()
    d_b2r = nc.dram_tensor("b2rep", [E, 128, D], F32, kind="ExternalInput").ap()
    d_identb = nc.dram_tensor("identb", [128, 128], BF16, kind="ExternalInput").ap()
    d_trif = nc.dram_tensor("trif", [128, 128], F32, kind="ExternalInput").ap()
    d_bandb = nc.dram_tensor("bandb", [128, 128], BF16, kind="ExternalInput").ap()
    d_onesf = nc.dram_tensor("onesf", [128, 128], F32, kind="ExternalInput").ap()

    d_out = nc.dram_tensor("out", [T, D], F32, kind="ExternalOutput").ap()
    scratch_kind = "ExternalOutput" if DEBUG else "Internal"
    d_bufd = nc.dram_tensor("bufd", [BUFROWS, D], BF16, kind=scratch_kind).ap()
    d_yd = nc.dram_tensor("yd", [NS, D], F32, kind=scratch_kind).ap()
    d_atd = nc.dram_tensor("atd", [D, T], BF16, kind=scratch_kind).ap()
    if DEBUG:
        d_rt = nc.dram_tensor("rtd", [128, 256], F32, kind="ExternalOutput").ap()
        d_nrm = nc.dram_tensor("nrm", [2, 2048], F32, kind="ExternalOutput").ap()
        d_xnd = nc.dram_tensor("xnd", [T, D], BF16, kind="ExternalOutput").ap()
        d_qkd = nc.dram_tensor("qkd", [2 * D, T], BF16, kind="ExternalOutput").ap()

    def sb(name, shape, dt):
        return es.enter_context(nc.sbuf_tensor(name, shape, dt))

    ar_x = sb("ar_x", [128, 16, 1024], F32)
    ar_xn = sb("ar_xn", [128, 16, 1024], BF16)
    ar_xnt = sb("ar_xnt", [128, 8, 2048], BF16)
    qt = sb("qt", [128, 8, 2048], BF16)
    kt = sb("kt", [128, 8, 2048], BF16)

    wr_sb = sb("wr_sb", [128, 8, 8], F32)
    wr_bf = sb("wr_bf", [128, 8, 8], BF16)
    b1_sb = sb("b1_sb", [128, E, 32], F32)
    identb = sb("identb_sb", [128, 128], BF16)
    trif = sb("trif_sb", [128, 128], F32)
    bandb = sb("bandb_sb", [128, 128], BF16)
    onesf = sb("onesf_sb", [128, 128], F32)

    s1 = sb("s1", [128, 16], F32)
    s2 = sb("s2", [128, 16], F32)
    mean = sb("mean", [128, 16], F32)
    ve = sb("ve", [128, 16], F32)
    srt = sb("srt", [128, 16], F32)
    rstd = sb("rstd", [128, 16], F32)
    negmr = sb("negmr", [128, 16], F32)
    sm = sb("sm", [128, 16], F32)
    rsum = sb("rsum", [128, 16], F32)
    e1 = sb("e1", [128, 16], F32)
    e2 = sb("e2", [128, 16], F32)
    p1 = sb("p1", [128, 16], F32)
    p2 = sb("p2", [128, 16], F32)
    toto = sb("toto", [1, 128], F32)
    offa = sb("offa", [1, 128], F32)
    offb = sb("offb", [1, 128], F32)
    pos1 = sb("pos1", [128, 16], F32)
    pos2 = sb("pos2", [128, 16], F32)
    keep1 = sb("keep1", [128, 16], F32)
    keep2 = sb("keep2", [128, 16], F32)
    pc1 = sb("pc1", [128, 16], F32)
    pc2 = sb("pc2", [128, 16], F32)
    st1 = sb("st1", [128, 16], F32)
    st2 = sb("st2", [128, 16], F32)
    w1c = sb("w1c", [128, 16], F32)
    w2c = sb("w2c", [128, 16], F32)
    sc1 = sb("sc1", [128, 16], I32)
    sc2 = sb("sc2", [128, 16], I32)
    gr1 = sb("gr1", [128, 16], I32)
    gr2 = sb("gr2", [128, 16], I32)
    zero_b = sb("zero_b", [128, 256], BF16)
    negc = sb("negc", [128, 1], F32)
    cns = sb("cns", [1, 16], F32)
    r_sb = sb("r_sb", [1, 512], F32)
    bc_sb = sb("bc_sb", [64, 512], F32)
    at_blk = sb("at_blk", [64, 2, 512], BF16)
    relu_sb = sb("relu_sb", [128, 2, 320], BF16)

    xa = ar_x[:]
    xa_b = ar_x[:].bitcast(BF16)
    av_flat = ar_x[:].rearrange("p a b -> p (a b)").bitcast(BF16)

    def xtile(t):
        return xa[:, t, :]

    wst_f32 = [xa[:, 0, :].rearrange("p (k h) -> p k h", k=8),
               xa[:, 1, :].rearrange("p (k h) -> p k h", k=8)]
    wst_bf = [xa_b[:, 2, 0:1024].rearrange("p (k h) -> p k h", k=8),
              xa_b[:, 2, 1024:2048].rearrange("p (k h) -> p k h", k=8)]
    wv_f32 = ar_x[:, 0:4, :].rearrange("p a b -> p (a b)").rearrange(
        "p (k h) -> p k h", k=8)
    wv_bf = ar_x[:, 4:6, :].bitcast(BF16).rearrange("p a b -> p (a b)").rearrange(
        "p (k h) -> p k h", k=8)
    VAUG0 = 6 * 2048
    w2st_f32 = [xa[:, 3, :], xa[:, 9, :]]
    w2st_bf = [xa_b[:, 13, 0:1024], xa_b[:, 13, 1024:2048]]
    ybufs = [[xa[:, 4, :], xa[:, 5, :]],
             [xa[:, 6, :], xa[:, 7, :]],
             [xa[:, 8, :], xa[:, 10, :]]]
    b2buf = xa[:, 15, :]
    sqbuf = [xa[:, 0, :], xa[:, 1, :]]
    g1_sb = xa[:, 4, :]
    g2_sb = xa[:, 5, :]
    xr_sb = [xa[:, 6, :], xa[:, 7, :]]
    boutr_sb = xa[:, 8, :]
    outacc = [xa[:, 13, :], xa[:, 14, :]]
    t1_sb = xa[:, 0, :]
    t2_sb = xa[:, 1, :]
    woutT_res = [xa_b[:, 9 + i, :] for i in range(4)]

    def wout_tile(m):
        return woutT_res[m // 2][:, (m % 2) * 1024:(m % 2) * 1024 + 1024].rearrange(
            "p (k h) -> p k h", k=8)

    def xn_tile(t):
        return ar_xn[:, t, :]

    xn_flat = ar_xn[:].rearrange("p a b -> p (a b)")

    def pt_blk(j):
        return xn_flat[:, j * 512:(j + 1) * 512]

    xnt = ar_xnt[:]
    xnt_flat = ar_xnt[:].rearrange("p a b -> p (a b)")

    def buft(par, k):
        base = par * 2560 + k * 320
        return xnt_flat[:, base:base + 320]

    qt_f32 = qt[:].bitcast(F32)
    kt_f32 = kt[:].bitcast(F32)
    ex = kt_f32[:, 6, 0:128]
    m8a = kt_f32[:, 6, 128:256]
    i8a = kt_f32[:, 6, 256:384].bitcast(U32)
    o1 = kt_f32[:, 6, 384:512]
    o2 = kt_f32[:, 6, 512:640]
    oo = kt_f32[:, 6, 640:768]
    posw = kt_f32[:, 6, 768:896]
    pos = kt_f32[:, 6, 896:1024]
    tmpa = kt_f32[:, 7, 0:128]
    tmpb = kt_f32[:, 7, 128:256]

    psum = es.enter_context(nc.psum_tensor("psum", [128, 8, 512], F32))
    psA = psum[:]
    ptr_b = [psA[:, 0, 0:256].bitcast(BF16).rearrange("p (a b) -> p a b", a=4),
             psA[:, 1, 0:256].bitcast(BF16).rearrange("p (a b) -> p a b", a=4)]
    psR = psA[:, 2, 0:128]
    psPW = psA[:, 3, 0:128]
    psTot = psA[0:1, 4, 0:128]
    psOFF = psA[:, 5, 0:128]
    psQK = [psA[:, 0, :], psA[:, 1, :]]
    psV = [psA[:, 2, :], psA[:, 3, :]]
    psNq = psA[0:1, 4:8, :]
    psNk = psA[0:1, 0:4, :]
    psBC1 = psA[:, 6, 0:1]
    psS = psA[:, 0:4, :]
    psAV = psA[:, 4, :]
    psBC = psA[0:64, 5, :]
    psW1 = [psA[:, 6, 0:320], psA[:, 7, 0:320]]
    psY = [psA[:, b, :] for b in range(6)]
    psO = [psA[:, 6, :], psA[:, 7, :]]

    chans = [
        "LX", "LC", "SQ", "VX", "VVE", "ASQ", "VLN", "AXN", "TTR", "VTE",
        "TRT", "AEX", "VOO", "TCS", "VOF", "TC2", "VRT", "GZ", "GSC",
        "LQ", "VWQ", "TQK", "AQE", "LV", "VWV", "TVV", "AVE",
        "ASQ2", "TNM", "VNM", "ANM", "VNC", "TNC",
        "TSG", "AEXP", "VMSK", "TAV", "VR", "TBC", "VBC", "SAT",
        "VAT", "LBT", "LW1", "LW2", "VW1", "VW2", "TW1", "ARL", "TW2",
        "VYE", "SYD", "LB2", "LWO", "VWO", "LAT", "LXR", "GG", "TPO",
        "VCB", "SOUT", "VZ", "LMISC", "VWR",
    ]
    sems = {c: es.enter_context(nc.semaphore("s_" + c)) for c in chans}

    SCHED = [("attn", hh) for hh in range(H)] + [("ffn", e) for e in range(E)]

    def program(W):
        eng = W["eng"]
        h = W["h"]
        C = W["C"]

        def on(e):
            return eng == e

        def tick(inst, ch, n=1):
            C[ch] = C.get(ch, 0) + n
            if inst is not None:
                inst.then_inc(sems[ch], n)
            return C[ch]

        def cnt(ch):
            return C.get(ch, 0)

        def wait(e, ch, val):
            if on(e) and val > 0:
                h.wait_ge(sems[ch], val)

        for (dst, src) in [(identb[:], d_identb), (trif[:], d_trif),
                           (bandb[:], d_bandb), (onesf[:], d_onesf),
                           (wr_sb[:], d_wr), (b1_sb[:], d_b1t)]:
            i = h.dma_start(out=dst, in_=src) if on("S") else None
            tick(i, "LC", 16)
        for t in range(TT):
            i = h.dma_start(out=xtile(t), in_=d_x[128 * t:128 * (t + 1), :]) if on("S") else None
            tick(i, "LX", 16)

        i = h.memset(zero_b[:], 0.0) if on("V") else None
        tick(i, "VZ", 1)
        wait("V", "LC", 96)
        i = h.tensor_copy(wr_bf[:], wr_sb[:]) if on("V") else None
        tick(i, "VWR", 1)
        wait("G", "VZ", 1)
        i = None
        if on("G"):
            i = h.dma_start(out=d_bufd.rearrange("(a p) f -> a p f", p=128),
                            in_=zero_b[:].unsqueeze(1).broadcast_to(
                                [128, (BUFROWS * 1024) // (128 * 256), 256]))
        tick(i, "GZ", 16)

        for t in range(TT):
            wait("A", "LX", 16 * (t + 1))
            wait("A", "VX", max(0, t - 1))
            i = None
            if on("A"):
                i = h.activation(out=qt_f32[:, t % 2, :], in_=xtile(t), func=AF.Square)
            tick(i, "SQ", 1)
            wait("V", "LX", 16 * (t + 1))
            if on("V"):
                h.reduce_sum(s1[:, t:t + 1], xtile(t), axis=AX.X)
            wait("V", "SQ", t + 1)
            i = None
            if on("V"):
                i = h.reduce_sum(s2[:, t:t + 1], qt_f32[:, t % 2, :], axis=AX.X)
            tick(i, "VX", 1)
        i = None
        if on("V"):
            h.drain()
            h.tensor_scalar(mean[:], s1[:], 1.0 / D, None, OP.mult)
            h.drain()
            h.tensor_tensor(tmpa[:, 0:16], mean[:], mean[:], OP.mult)
            h.drain()
            h.scalar_tensor_tensor(ve[:], s2[:], 1.0 / D, tmpa[:, 0:16],
                                    OP.mult, OP.subtract)
            h.drain()
            i = h.tensor_scalar(ve[:], ve[:], 1e-5, None, OP.add)
        tick(i, "VVE", 1)
        wait("A", "VVE", 1)
        i = None
        if on("A"):
            h.drain()
            i = h.activation(out=srt[:], in_=ve[:], func=AF.Sqrt)
        tick(i, "ASQ", 1)
        wait("V", "ASQ", 1)
        i = None
        if on("V"):
            h.reciprocal(rstd[:], srt[:])
            h.drain()
            h.tensor_tensor(tmpa[:, 16:32], mean[:], rstd[:], OP.mult)
            h.drain()
            i = h.tensor_scalar(negmr[:], tmpa[:, 16:32], -1.0, None, OP.mult)
        tick(i, "VLN", 1)
        for t in range(TT):
            wait("A", "VLN", 1)
            i = None
            if on("A"):
                i = h.activation(out=xn_tile(t), in_=xtile(t), func=AF.Identity,
                                 bias=negmr[:, t:t + 1], scale=rstd[:, t:t + 1])
            tick(i, "AXN", 1)

        for t in range(TT):
            for g in range(2):
                gi = 2 * t + g
                wait("T", "AXN", t + 1)
                wait("T", "LC", 16)
                wait("T", "VTE", max(0, gi - 1))
                i = None
                if on("T"):
                    for c4 in range(4):
                        c = 4 * g + c4
                        h.transpose(ptr_b[gi % 2][:, c4, :],
                                    xn_tile(t)[:, 128 * c:128 * (c + 1)], identb[:])
                    i = h.drain()
                tick(i, "TTR", 1)
                wait("V", "TTR", gi + 1)
                i = None
                if on("V"):
                    i = h.tensor_copy(xnt[:, 4 * g:4 * g + 4, 128 * t:128 * (t + 1)],
                                      ptr_b[gi % 2][:])
                tick(i, "VTE", 1)

        wait("T", "VTE", 32)
        wait("T", "LC", 96)
        wait("T", "VWR", 1)
        i = None
        if on("T"):
            for t in range(TT):
                for k in range(CT):
                    i = h.matmul(psR[:, 8 * t:8 * t + 8],
                                 xnt[:, k, 128 * t:128 * (t + 1)], wr_bf[:, k, :],
                                 start=(k == 0), stop=(k == CT - 1))
        tick(i, "TRT", 1)
        wait("A", "TRT", 1)
        i = None
        if on("A"):
            i = h.activation(out=ex, in_=psR, func=AF.Exp)
        tick(i, "AEX", 1)
        wait("V", "AEX", 1)
        i = None
        if on("V"):
            exv = ex.rearrange("p (g e) -> p g e", e=8)
            h.reduce_sum(sm[:], exv, axis=AX.X)
            h.drain()
            h.reciprocal(rsum[:], sm[:])
            for t in range(TT):
                h.max(m8a[:, 8 * t:8 * t + 8], ex[:, 8 * t:8 * t + 8])
            h.drain()
            for t in range(TT):
                h.max_index(i8a[:, 8 * t:8 * t + 8], m8a[:, 8 * t:8 * t + 8],
                            ex[:, 8 * t:8 * t + 8])
            h.drain()
            m8v = m8a.rearrange("p (g e) -> p g e", e=8)
            i8v = i8a.rearrange("p (g e) -> p g e", e=8)
            h.tensor_copy(e1[:], i8v[:, :, 0])
            h.tensor_copy(e2[:], i8v[:, :, 1])
            h.tensor_tensor(p1[:], m8v[:, :, 0], rsum[:], OP.mult)
            h.tensor_tensor(p2[:], m8v[:, :, 1], rsum[:], OP.mult)
            h.drain()
            o1v = o1.rearrange("p (g e) -> p g e", e=8)
            o2v = o2.rearrange("p (g e) -> p g e", e=8)
            for j in range(8):
                h.tensor_scalar(o1v[:, :, j], e1[:], float(j), None, OP.is_equal)
                h.tensor_scalar(o2v[:, :, j], e2[:], float(j), None, OP.is_equal)
            h.drain()
            i = h.tensor_tensor(oo, o1, o2, OP.add)
        tick(i, "VOO", 1)
        wait("T", "VOO", 1)
        i = None
        if on("T"):
            for t in range(TT):
                i = h.matmul(psPW[:, 8 * t:8 * t + 8], trif[:],
                             oo[:, 8 * t:8 * t + 8], start=True, stop=True)
                i = h.matmul(psTot[:, 8 * t:8 * t + 8], onesf[:, 0:1],
                             oo[:, 8 * t:8 * t + 8], start=True, stop=True)
        tick(i, "TCS", 1)
        wait("V", "TCS", 1)
        i = None
        if on("V"):
            h.tensor_copy(toto[:], psTot)
            h.drain()
            h.memset(offa[:, 0:8], 0.0)
            h.tensor_copy(offa[:, 8:128], toto[:, 0:120])
            h.drain()
            bufs = [offa, offb]
            cur = 0
            for s in (8, 16, 32, 64):
                src, dst = bufs[cur], bufs[1 - cur]
                h.tensor_copy(dst[:, 0:s], src[:, 0:s])
                h.tensor_tensor(dst[:, s:128], src[:, s:128], src[:, 0:128 - s], OP.add)
                h.drain()
                cur = 1 - cur
            i = h.engine_nop()
        tick(i, "VOF", 1)
        wait("T", "VOF", 1)
        i = None
        if on("T"):
            i = h.matmul(psOFF, onesf[0:1, :], offa[:], start=True, stop=True)
        tick(i, "TC2", 1)
        wait("V", "TC2", 1)
        i = None
        if on("V"):
            h.tensor_copy(posw, psPW)
            h.drain()
            h.tensor_tensor(pos, posw, psOFF, OP.add)
            h.drain()
            h.tensor_tensor(tmpa, pos, o1, OP.mult)
            h.tensor_tensor(tmpb, pos, o2, OP.mult)
            h.drain()
            h.reduce_sum(pos1[:], tmpa.rearrange("p (g e) -> p g e", e=8), axis=AX.X)
            h.reduce_sum(pos2[:], tmpb.rearrange("p (g e) -> p g e", e=8), axis=AX.X)
            h.drain()
            h.tensor_scalar(keep1[:], pos1[:], float(CAP), None, OP.is_lt)
            h.tensor_scalar(keep2[:], pos2[:], float(CAP), None, OP.is_lt)
            h.tensor_scalar(pc1[:], pos1[:], float(CAP - 1), None, OP.min)
            h.tensor_scalar(pc2[:], pos2[:], float(CAP - 1), None, OP.min)
            h.drain()
            h.tensor_scalar(tmpa[:, 0:16], e1[:], float(CAP), None, OP.mult)
            h.tensor_scalar(tmpa[:, 16:32], e2[:], float(CAP), None, OP.mult)
            h.tensor_tensor(w1c[:], p1[:], keep1[:], OP.mult)
            h.tensor_tensor(w2c[:], p2[:], keep2[:], OP.mult)
            h.drain()
            h.tensor_tensor(st1[:], tmpa[:, 0:16], pc1[:], OP.add)
            h.tensor_tensor(st2[:], tmpa[:, 16:32], pc2[:], OP.add)
            h.drain()
            h.tensor_copy(gr1[:], st1[:])
            h.tensor_copy(gr2[:], st2[:])
            h.tensor_scalar(tmpa[:, 32:48], st1[:], -1.0, float(NS), OP.mult, OP.add)
            h.tensor_scalar(tmpa[:, 48:64], st2[:], -1.0, float(NS), OP.mult, OP.add)
            h.tensor_scalar(tmpa[:, 64:80], keep1[:], -1.0, 1.0, OP.mult, OP.add)
            h.tensor_scalar(tmpa[:, 80:96], keep2[:], -1.0, 1.0, OP.mult, OP.add)
            h.drain()
            h.tensor_tensor(tmpb[:, 0:16], tmpa[:, 32:48], tmpa[:, 64:80], OP.mult)
            h.tensor_tensor(tmpb[:, 16:32], tmpa[:, 48:64], tmpa[:, 80:96], OP.mult)
            h.drain()
            h.tensor_tensor(tmpa[:, 96:112], st1[:], tmpb[:, 0:16], OP.add)
            h.tensor_tensor(tmpa[:, 112:128], st2[:], tmpb[:, 16:32], OP.add)
            h.drain()
            h.tensor_copy(sc1[:], tmpa[:, 96:112])
            h.tensor_copy(sc2[:], tmpa[:, 112:128])
            h.drain()
            i = h.engine_nop()
        tick(i, "VRT", 1)

        wait("G", "VRT", 1)
        wait("G", "AXN", TT)
        wait("G", "GZ", 16)
        for t in range(TT):
            for sc in (sc1, sc2):
                i = None
                if on("G"):
                    i = h.indirect_dma_start(
                        out=d_bufd,
                        out_offset=IndirectOffsetOnAxis(sc[:, t:t + 1], axis=0),
                        in_=xn_tile(t), in_offset=None)
                tick(i, "GSC", 16)
        GSC_ALL = cnt("GSC")

        for m in range(16):
            wait("S", "VWQ", max(0, m - 1))
            if m < 2:
                wait("S", "VX", TT)
                wait("S", "AXN", TT)
            i = None
            if on("S"):
                i = h.dma_start(out=wst_f32[m % 2][:],
                                in_=d_wqkvT[:, 128 * m:128 * (m + 1)].rearrange(
                                    "(k p) h -> p k h", p=128))
            tick(i, "LQ", 16)
            wait("V", "LQ", 16 * (m + 1))
            wait("V", "TQK", max(0, QC * (m - 1)))
            i = None
            if on("V"):
                i = h.tensor_copy(wst_bf[m % 2][:], wst_f32[m % 2][:])
            tick(i, "VWQ", 1)
            for ch in range(QC):
                q4 = QC * m + ch
                wait("T", "VWQ", m + 1)
                wait("T", "AQE", max(0, q4 - 1))
                i = None
                if on("T"):
                    for k in range(CT):
                        i = h.matmul(psQK[q4 % 2], wst_bf[m % 2][:, k, :],
                                     xnt[:, k, 512 * ch:512 * (ch + 1)],
                                     start=(k == 0), stop=(k == CT - 1))
                tick(i, "TQK", 1)
                wait("A", "TQK", q4 + 1)
                if m >= 14 and ch == 0:
                    wait("A", "VRT", 1)
                i = None
                if on("A"):
                    dst = qt if m < 8 else kt
                    i = h.activation(out=dst[:, m % 8, 512 * ch:512 * (ch + 1)],
                                     in_=psQK[q4 % 2], func=AF.Copy)
                tick(i, "AQE", 1)
        for vc in range(2):
            wait("S", "VWQ", 16)
            wait("S", "TQK", 64)
            wait("S", "TVV", 16 * vc)
            wait("S", "VWV", vc)
            wait("S", "AXN", TT)
            i = None
            if on("S"):
                i = h.dma_start(out=wv_f32,
                                in_=d_wqkvT[:, 2048 + 512 * vc:2048 + 512 * (vc + 1)]
                                .rearrange("(k p) h -> p k h", p=128))
            tick(i, "LV", 16)
            wait("V", "LV", 16 * (vc + 1))
            wait("V", "TVV", 16 * vc)
            i = None
            if on("V"):
                i = h.tensor_copy(wv_bf, wv_f32)
            tick(i, "VWV", 1)
            for tt_ in range(TT):
                vidx = 16 * vc + tt_
                wait("T", "VWV", vc + 1)
                wait("T", "AVE", max(0, vidx - 1))
                if vidx == 0:
                    wait("T", "VRT", 1)
                    wait("T", "AEX", 1)
                i = None
                if on("T"):
                    for k in range(CT):
                        i = h.matmul(psV[vidx % 2], xnt[:, k, 128 * tt_:128 * (tt_ + 1)],
                                     wv_bf[:, k, :], start=(k == 0), stop=(k == CT - 1))
                tick(i, "TVV", 1)
                wait("A", "TVV", vidx + 1)
                i = None
                if on("A"):
                    base = VAUG0 + tt_ * 1040 + (8 * vc) * 65
                    outap = av_flat[:, base:base + 8 * 65].rearrange(
                        "p (hd c) -> p hd c", hd=8)[:, :, 0:64]
                    i = h.activation(out=outap, in_=psV[vidx % 2].rearrange(
                        "p (hd c) -> p hd c", hd=8), func=AF.Copy)
                tick(i, "AVE", 1)
        wait("V", "AVE", 32)
        i = None
        if on("V"):
            for jt in range(TT):
                base = VAUG0 + jt * 1040 + 64
                i = h.memset(av_flat[:, base:base + 16 * 65].rearrange(
                    "p (hd c) -> p hd c", c=65)[:, :, 0:1], 1.0)
        tick(i, "VAT", 1)

        for i_ in range(32):
            src = qt if i_ < 16 else kt
            ct_, hf = (i_ % 16) // 2, i_ % 2
            if i_ == 0:
                wait("A", "TVV", 32)
            wait("A", "TNM", max(0, i_ - 1))
            i = None
            if on("A"):
                if i_ == 0:
                    h.drain()
                i = h.activation(out=sqbuf[i_ % 2],
                                 in_=src[:, ct_, 1024 * hf:1024 * (hf + 1)],
                                 func=AF.Square)
            tick(i, "ASQ2", 1)
            wait("T", "ASQ2", i_ + 1)
            if i_ == 0:
                wait("T", "VRT", 1)
            if i_ == 16:
                wait("T", "AVE", 32)
            i = None
            if on("T"):
                ps = psNq if i_ < 16 else psNk
                for half in range(2):
                    i = h.matmul(ps[:, 2 * hf + half, :], onesf[:, 0:1],
                                 sqbuf[i_ % 2][:, 512 * half:512 * (half + 1)],
                                 start=(ct_ == 0), stop=(ct_ == 7))
            tick(i, "TNM", 1)
        wait("V", "TNM", 32)
        i = None
        if on("V"):
            h.reduce_max(cns[:, 0:1], psNq.rearrange("p a b -> p (a b)"), axis=AX.X)
            h.reduce_max(cns[:, 1:2], psNk.rearrange("p a b -> p (a b)"), axis=AX.X)
            if DEBUG:
                h.tensor_copy(r_sb[:], psNq[:, 0, :])
            h.drain()
            i = h.engine_nop()
        tick(i, "VNM", 1)
        if DEBUG:
            i = None
            if on("V"):
                h.tensor_copy(bc_sb[0:1, :], psNk[:, 0, :])
                h.drain()
                i = h.engine_nop()
            tick(i, "VNM", 1)
            wait("S", "VNM", cnt("VNM"))
            i = None
            if on("S"):
                i = h.dma_start(out=d_nrm[0:1, 0:512], in_=r_sb[:])
            tick(i, "LMISC", 16)
            i = None
            if on("S"):
                i = h.dma_start(out=d_nrm[1:2, 0:512], in_=bc_sb[0:1, :])
            tick(i, "LMISC", 16)
        wait("A", "VNM", 1)
        i = None
        if on("A"):
            h.drain()
            i = h.activation(out=cns[:, 2:4], in_=cns[:, 0:2], func=AF.Sqrt)
        tick(i, "ANM", 1)
        wait("V", "ANM", 1)
        i = None
        if on("V"):
            h.drain()
            h.tensor_tensor(cns[:, 4:5], cns[:, 2:3], cns[:, 3:4], OP.mult)
            h.drain()
            h.tensor_scalar(cns[:, 5:6], cns[:, 4:5], 0.125, None, OP.mult)
            h.drain()
            h.tensor_scalar(cns[:, 6:7], cns[:, 5:6], -44.0, 0.0, OP.add, OP.max)
            h.drain()
            i = h.tensor_scalar(cns[:, 7:8], cns[:, 6:7], -1.0, None, OP.mult)
        tick(i, "VNC", 1)
        wait("T", "VNC", 1)
        i = None
        if on("T"):
            i = h.matmul(psBC1, onesf[0:1, :], cns[0:1, 7:8], start=True, stop=True)
        tick(i, "TNC", 1)
        wait("V", "TNC", 1)
        i = None
        if on("V"):
            i = h.tensor_copy(negc[:], psBC1)
        tick(i, "VNC", 1)

        if DEBUG:
            wait("S", "AQE", 64)
            wait("S", "AVE", 32)
            for ct_ in range(8):
                i = None
                if on("S"):
                    i = h.dma_start(out=d_qkd[128 * ct_:128 * (ct_ + 1), :],
                                    in_=qt[:, ct_, :])
                tick(i, "LMISC", 16)
                i = None
                if on("S"):
                    i = h.dma_start(out=d_qkd[1024 + 128 * ct_:1024 + 128 * (ct_ + 1), :],
                                    in_=kt[:, ct_, :])
                tick(i, "LMISC", 16)

        at_cnt = 0
        first_exp = True
        for kind, idx in SCHED:
            if kind == "attn":
                hh = idx
                ct2, pb = hh // 2, 64 * (hh % 2)
                for q in range(QC):
                    ngrp = q + 1
                    for g in range(ngrp):
                        wait("T", "AEXP", cnt("AEXP"))
                        if hh == 0 and q == 0 and g == 0:
                            wait("T", "AQE", 64)
                            wait("T", "AVE", 32)
                            wait("T", "VNC", 2)
                            wait("T", "VNM", 1)
                        i = None
                        if on("T"):
                            for j4 in range(4):
                                j = 4 * g + j4
                                i = h.matmul(psS[:, j4, :],
                                             kt[pb:pb + 64, ct2, 128 * j:128 * (j + 1)],
                                             qt[pb:pb + 64, ct2, 512 * q:512 * (q + 1)],
                                             start=True, stop=True)
                        tick(i, "TSG", 1)
                        wait("A", "TSG", cnt("TSG"))
                        wait("A", "TAV", cnt("TAV"))
                        if first_exp:
                            wait("A", "GSC", GSC_ALL)
                            first_exp = False
                        i = None
                        if on("A"):
                            i = h.activation(
                                out=xn_flat[:, 4 * g * 512:(4 * g + 4) * 512],
                                in_=psS.rearrange("p a b -> p (a b)"),
                                func=AF.Exp, bias=negc[:, 0:1], scale=0.125)
                        tick(i, "AEXP", 1)
                    wait("V", "AEXP", cnt("AEXP"))
                    i = None
                    if on("V"):
                        for m_ in range(4):
                            j = 4 * q + m_
                            if m_ > 0:
                                h.memset(pt_blk(j)[:, 0:128 * m_], 0.0)
                            h.tensor_tensor(pt_blk(j)[:, 128 * m_:128 * (m_ + 1)],
                                            pt_blk(j)[:, 128 * m_:128 * (m_ + 1)],
                                            bandb[:], OP.mult)
                        i = h.engine_nop()
                    tick(i, "VMSK", 1)
                    jmax = 4 * (q + 1)
                    wait("T", "VMSK", cnt("VMSK"))
                    wait("T", "VAT", 1)
                    wait("T", "VBC", cnt("VBC"))
                    i = None
                    if on("T"):
                        for j in range(jmax):
                            base = VAUG0 + j * 1040 + hh * 65
                            i = h.matmul(psAV[0:65, :], av_flat[:, base:base + 65],
                                         pt_blk(j), start=(j == 0), stop=(j == jmax - 1))
                    tick(i, "TAV", 1)
                    wait("V", "TAV", cnt("TAV"))
                    i = None
                    if on("V"):
                        i = h.reciprocal(r_sb[:], psAV[64:65, :])
                    tick(i, "VR", 1)
                    wait("T", "VR", cnt("VR"))
                    i = None
                    if on("T"):
                        i = h.matmul(psBC, onesf[0:1, 0:64], r_sb[:], start=True, stop=True)
                    tick(i, "TBC", 1)
                    wait("V", "TBC", cnt("TBC"))
                    wait("V", "SAT", 16 * max(0, at_cnt - 1))
                    i = None
                    if on("V"):
                        h.drain()
                        h.tensor_copy(bc_sb[:], psBC)
                        h.drain()
                        i = h.tensor_tensor(at_blk[:, at_cnt % 2, :], psAV[0:64, :],
                                            bc_sb[:], OP.mult)
                    tick(i, "VBC", 1)
                    wait("S", "VBC", cnt("VBC"))
                    i = None
                    if on("S"):
                        i = h.dma_start(
                            out=d_atd[64 * hh:64 * (hh + 1), 512 * q:512 * (q + 1)],
                            in_=at_blk[:, at_cnt % 2, :])
                    tick(i, "SAT", 16)
                    at_cnt += 1
            else:
                e = idx
                wait("S", "GSC", GSC_ALL)
                if e < 2:
                    wait("S", "TVV", 32)
                    wait("S", "TRT", 1)
                    wait("S", "TQK", 64)
                else:
                    wait("S", "TW1", 32 * (e - 1))
                for c in range(CT):
                    i = None
                    if on("S"):
                        i = h.dma_start(out=buft(e % 2, c),
                                        in_=d_bufd[CAP * e:CAP * (e + 1),
                                                   128 * c:128 * (c + 1)],
                                        transpose=True)
                    tick(i, "LBT", 16)
                wait("S", "VYE", cnt("VYE"))
                i = None
                if on("S"):
                    i = h.dma_start(out=b2buf, in_=d_b2r[e])
                tick(i, "LB2", 16)
                for ht in range(32):
                    gidx = 32 * e + ht
                    wait("S", "VW1", max(0, gidx - 1))
                    if gidx == 0:
                        wait("S", "VWQ", 16)
                        wait("S", "TVV", 32)
                        wait("S", "TNM", 32)
                        wait("S", "VWV", 2)
                    i = None
                    if on("S"):
                        i = h.dma_start(out=wst_f32[gidx % 2][:],
                                        in_=d_w1[e, :, 128 * ht:128 * (ht + 1)].rearrange(
                                            "(k p) hd -> p k hd", p=128))
                    tick(i, "LW1", 16)
                    wait("S", "VW2", max(0, gidx - 1))
                    if gidx <= 1:
                        wait("S", "TAV", 64)
                    i = None
                    if on("S"):
                        i = h.dma_start(out=w2st_f32[gidx % 2],
                                        in_=d_w2[e, 128 * ht:128 * (ht + 1), :])
                    tick(i, "LW2", 16)
                    wait("V", "LW1", 16 * (gidx + 1))
                    wait("V", "TW1", max(0, cnt("TW1") - 1))
                    i = None
                    if on("V"):
                        i = h.tensor_copy(wst_bf[gidx % 2][:], wst_f32[gidx % 2][:])
                    tick(i, "VW1", 1)
                    wait("V", "LW2", 16 * (gidx + 1))
                    wait("V", "TW2", max(0, cnt("TW2") - 6))
                    i = None
                    if on("V"):
                        i = h.tensor_copy(w2st_bf[gidx % 2], w2st_f32[gidx % 2])
                    tick(i, "VW2", 1)
                    wait("T", "VW1", gidx + 1)
                    wait("T", "LBT", 16 * 8 * (e + 1))
                    wait("T", "ARL", max(0, cnt("ARL") - 1))
                    if gidx == 0:
                        wait("T", "VNC", 2)
                        wait("T", "VBC", cnt("VBC"))
                    i = None
                    if on("T"):
                        for k in range(CT):
                            i = h.matmul(psW1[ht % 2], wst_bf[gidx % 2][:, k, :],
                                         buft(e % 2, k), start=(k == 0),
                                         stop=(k == CT - 1))
                    tick(i, "TW1", 1)
                    wait("A", "TW1", cnt("TW1"))
                    wait("A", "TW2", max(0, cnt("TW2") - 6))
                    i = None
                    if on("A"):
                        i = h.activation(out=relu_sb[:, ht % 2, :], in_=psW1[ht % 2],
                                         func=AF.Relu, bias=b1_sb[:, e, ht:ht + 1])
                    tick(i, "ARL", 1)
                    wait("T", "ARL", cnt("ARL"))
                    wait("T", "VW2", gidx + 1)
                    if ht == 0:
                        wait("T", "VYE", cnt("VYE"))
                    i = None
                    if on("T"):
                        for sc_ in range(3):
                            rows = 64 if sc_ == 2 else 128
                            for cc in range(2):
                                i = h.matmul(psY[2 * sc_ + cc][0:rows, :],
                                             relu_sb[:, ht % 2, 128 * sc_:128 * sc_ + rows],
                                             w2st_bf[gidx % 2][:, 512 * cc:512 * (cc + 1)],
                                             start=(ht == 0), stop=(ht == 31))
                    tick(i, "TW2", 6)
                wait("V", "TW2", cnt("TW2"))
                wait("V", "LB2", cnt("LB2"))
                if e == 0:
                    wait("V", "TAV", 64)
                wait("V", "SYD", 16 * 3 * max(0, e - 1))
                i = None
                if on("V"):
                    for sc_ in range(3):
                        rows = 64 if sc_ == 2 else 128
                        ybuf = ybufs[sc_][e % 2]
                        for cc in range(2):
                            h.tensor_tensor(ybuf[0:rows, 512 * cc:512 * (cc + 1)],
                                            psY[2 * sc_ + cc][0:rows, :],
                                            b2buf[0:rows, 512 * cc:512 * (cc + 1)],
                                            OP.add)
                    i = h.engine_nop()
                tick(i, "VYE", 1)
                wait("S", "VYE", cnt("VYE"))
                for sc_ in range(3):
                    rows = 64 if sc_ == 2 else 128
                    ybuf = ybufs[sc_][e % 2]
                    i = None
                    if on("S"):
                        i = h.dma_start(
                            out=d_yd[CAP * e + 128 * sc_:CAP * e + 128 * sc_ + rows, :],
                            in_=ybuf[0:rows, :])
                    tick(i, "SYD", 16)
        SYD_ALL = cnt("SYD")
        SAT_ALL = cnt("SAT")

        for m in range(8):
            wait("S", "VWO", max(0, m - 1))
            if m < 2:
                wait("S", "VW1", 256)
            i = None
            if on("S"):
                i = h.dma_start(out=wst_f32[m % 2][:],
                                in_=d_woutT[:, 128 * m:128 * (m + 1)].rearrange(
                                    "(k p) hd -> p k hd", p=128))
            tick(i, "LWO", 16)
            wait("V", "LWO", 16 * (m + 1))
            if m == 0:
                wait("V", "SYD", SYD_ALL)
            i = None
            if on("V"):
                i = h.tensor_copy(wout_tile(m), wst_f32[m % 2][:])
            tick(i, "VWO", 1)
        i = h.dma_start(out=boutr_sb, in_=d_boutr) if on("S") else None
        tick(i, "LMISC", 16)

        TSG_ALL = cnt("TSG")
        for t in range(TT):
            if t == 0:
                wait("S", "TSG", TSG_ALL)
            wait("S", "TPO", 2 * max(0, t - 1))
            i = None
            if on("S"):
                i = h.dma_start(out=kt[:, 0:8, 128 * (t % 2):128 * (t % 2) + 128],
                                in_=d_atd[:, 128 * t:128 * (t + 1)].rearrange(
                                    "(k p) c -> p k c", p=128))
            tick(i, "LAT", 16)
            wait("S", "VCB", max(0, cnt("VCB") - 1))
            i = None
            if on("S"):
                i = h.dma_start(out=xr_sb[t % 2], in_=d_x[128 * t:128 * (t + 1), :])
            tick(i, "LXR", 16)
            wait("G", "SYD", SYD_ALL)
            wait("G", "VCB", cnt("VCB"))
            i = None
            if on("G"):
                i = h.indirect_dma_start(out=g1_sb, out_offset=None, in_=d_yd,
                                         in_offset=IndirectOffsetOnAxis(
                                             gr1[:, t:t + 1], axis=0))
            tick(i, "GG", 16)
            i = None
            if on("G"):
                i = h.indirect_dma_start(out=g2_sb, out_offset=None, in_=d_yd,
                                         in_offset=IndirectOffsetOnAxis(
                                             gr2[:, t:t + 1], axis=0))
            tick(i, "GG", 16)
            wait("T", "LAT", cnt("LAT"))
            wait("T", "VWO", 8)
            wait("T", "VCB", cnt("VCB"))
            if t == 0:
                wait("T", "ARL", 256)
            i = None
            if on("T"):
                for m in range(8):
                    for k in range(CT):
                        i = h.matmul(psO[m // 4][:, 128 * (m % 4):128 * (m % 4) + 128],
                                     kt[:, k, 128 * (t % 2):128 * (t % 2) + 128],
                                     wout_tile(m)[:, k, :],
                                     start=(k == 0), stop=(k == CT - 1))
            tick(i, "TPO", 2)
            wait("V", "TPO", cnt("TPO"))
            wait("V", "GG", cnt("GG"))
            wait("V", "LXR", 16 * (t + 1))
            wait("V", "LMISC", 16)
            wait("V", "SOUT", 16 * max(0, t - 1))
            i = None
            if on("V"):
                h.drain()
                h.tensor_scalar(t1_sb, g1_sb, w1c[:, t:t + 1], None, OP.mult)
                h.drain()
                h.scalar_tensor_tensor(t2_sb, g2_sb, w2c[:, t:t + 1], t1_sb,
                                       OP.mult, OP.add)
                h.drain()
                for cc in range(2):
                    h.tensor_tensor(t1_sb[:, 512 * cc:512 * (cc + 1)], psO[cc],
                                    t2_sb[:, 512 * cc:512 * (cc + 1)], OP.add)
                h.drain()
                h.tensor_tensor(t2_sb, xr_sb[t % 2], boutr_sb, OP.add)
                h.drain()
                i = h.tensor_tensor(outacc[t % 2], t1_sb, t2_sb, OP.add)
            tick(i, "VCB", 1)
            wait("S", "VCB", cnt("VCB"))
            i = None
            if on("S"):
                i = h.dma_start(out=d_out[128 * t:128 * (t + 1), :], in_=outacc[t % 2])
            tick(i, "SOUT", 16)
        if DEBUG:
            wait("V", "VCB", cnt("VCB"))
            i = None
            if on("V"):
                h.drain()
                rtd_sb = outacc[1]
                h.tensor_copy(rtd_sb[:, 0:16], p1[:])
                h.tensor_copy(rtd_sb[:, 16:32], e1[:])
                h.tensor_copy(rtd_sb[:, 32:48], pos1[:])
                h.tensor_copy(rtd_sb[:, 48:64], w1c[:])
                h.tensor_copy(rtd_sb[:, 64:80], keep1[:])
                h.tensor_copy(rtd_sb[:, 80:96], st1[:])
                h.tensor_copy(rtd_sb[:, 96:112], gr1[:])
                h.tensor_copy(rtd_sb[:, 112:128], p2[:])
                h.tensor_copy(rtd_sb[:, 128:144], e2[:])
                h.tensor_copy(rtd_sb[:, 144:160], pos2[:])
                h.tensor_copy(rtd_sb[:, 160:176], w2c[:])
                h.tensor_copy(rtd_sb[:, 176:192], gr2[:])
                h.tensor_copy(rtd_sb[0:1, 192:208], cns[:])
                h.tensor_copy(rtd_sb[:, 208:209], negc[:])
                h.drain()
                i = h.engine_nop()
            tick(i, "VCB", 1)
            wait("S", "VCB", cnt("VCB"))
            i = None
            if on("S"):
                i = h.dma_start(out=d_rt[:, 0:256], in_=outacc[1][:, 0:256])
            tick(i, "SOUT", 16)
        if on("G"):
            h.wait_ge(sems["SOUT"], cnt("SOUT"))

    block = es.enter_context(nc.Block())

    def mk(eng):
        def f(hh):
            program({"eng": eng, "h": hh, "C": {}})
        return f

    block.sync(mk("S"))
    block.gpsimd(mk("G"))
    block.vector(mk("V"))
    block.scalar(mk("A"))
    block.tensor(mk("T"))
    es.close()
    return nc


_NC_CACHE = {}


def _get_nc():
    if "nc" not in _NC_CACHE:
        _NC_CACHE["nc"] = build_core()
    return _NC_CACHE["nc"]


def kernel(L0_attn_x, L0_attn_Wqkv, L0_attn_Wout, L0_attn_bout,
           L0_ffn_Wr, L0_ffn_W1, L0_ffn_b1, L0_ffn_W2, L0_ffn_b2):
    x = np.asarray(L0_attn_x, dtype=np.float32)
    wqkv = np.asarray(L0_attn_Wqkv, dtype=np.float32)
    wout = np.asarray(L0_attn_Wout, dtype=np.float32)
    bout = np.asarray(L0_attn_bout, dtype=np.float32)
    wr = np.asarray(L0_ffn_Wr, dtype=np.float32)
    w1 = np.ascontiguousarray(np.asarray(L0_ffn_W1, dtype=np.float32))
    b1 = np.asarray(L0_ffn_b1, dtype=np.float32)
    w2 = np.ascontiguousarray(np.asarray(L0_ffn_W2, dtype=np.float32))
    b2 = np.asarray(L0_ffn_b2, dtype=np.float32)

    wqkvT = np.ascontiguousarray(wqkv.T)
    woutT = np.ascontiguousarray(wout.T)
    boutrep = np.ascontiguousarray(np.broadcast_to(bout[None, :], (128, D)))
    wrp = np.ascontiguousarray(wr.reshape(8, 128, 8).transpose(1, 0, 2))
    b1t = np.ascontiguousarray(b1.reshape(E, 32, 128).transpose(2, 0, 1))
    b2rep = np.ascontiguousarray(np.broadcast_to(b2[:, None, :], (E, 128, D)))
    identb = np.eye(128, dtype=ml_dtypes.bfloat16)
    trif = np.triu(np.ones((128, 128), np.float32), 1)
    bandb = np.triu(np.ones((128, 128), np.float32), 0).astype(ml_dtypes.bfloat16)
    onesf = np.ones((128, 128), np.float32)

    shared = dict(wqkvT=wqkvT, woutT=woutT, boutrep=boutrep, wrp=wrp,
                  w1=w1, w2=w2, b1t=b1t, b2rep=b2rep, identb=identb,
                  trif=trif, bandb=bandb, onesf=onesf)
    in_maps = [dict(shared, x=np.ascontiguousarray(x[b])) for b in range(8)]

    nc = _get_nc()
    res = run_bass_kernel_spmd(nc, in_maps, core_ids=list(range(8)))
    out = np.stack([r["out"] for r in res.results], axis=0)
    return out


# version 12
# speedup vs baseline: 1.2501x; 1.2501x over previous
import numpy as np
import ml_dtypes
from contextlib import ExitStack

import concourse.bass as bass
import concourse.mybir as mybir
from concourse.bass import IndirectOffsetOnAxis
from concourse.bass_utils import run_bass_kernel_spmd

F32 = mybir.dt.float32
BF16 = mybir.dt.bfloat16
I32 = mybir.dt.int32
U32 = mybir.dt.uint32
AF = mybir.ActivationFunctionType
OP = mybir.AluOpType
AX = mybir.AxisListType

T, D, H, DH, E, CAP, HID = 2048, 1024, 16, 64, 8, 320, 4096
TT, CT, QC = T // 128, D // 128, T // 512
NS = E * CAP
BUFROWS = 2688
DEBUG = False


def build_core():
    nc = bass.Bass("TRN2", target_bir_lowering=False)
    es = ExitStack()

    d_x = nc.dram_tensor("x", [T, D], F32, kind="ExternalInput").ap()
    d_wqkvp = nc.dram_tensor("wqkvp", [16, 128, 8, 128], F32, kind="ExternalInput").ap()
    d_wvp = nc.dram_tensor("wvp", [2, 128, 8, 512], F32, kind="ExternalInput").ap()
    d_woutp = nc.dram_tensor("woutp", [8, 128, 8, 128], F32, kind="ExternalInput").ap()
    d_boutr = nc.dram_tensor("boutrep", [128, D], F32, kind="ExternalInput").ap()
    d_wr = nc.dram_tensor("wrp", [128, 8, 8], F32, kind="ExternalInput").ap()
    d_w1p = nc.dram_tensor("w1p", [E, 32, 128, 8, 128], F32, kind="ExternalInput").ap()
    d_w2 = nc.dram_tensor("w2", [E, HID, D], F32, kind="ExternalInput").ap()
    d_b1t = nc.dram_tensor("b1t", [128, E, 32], F32, kind="ExternalInput").ap()
    d_b2r = nc.dram_tensor("b2rep", [E, 128, D], F32, kind="ExternalInput").ap()
    d_identb = nc.dram_tensor("identb", [128, 128], BF16, kind="ExternalInput").ap()
    d_trif = nc.dram_tensor("trif", [128, 128], F32, kind="ExternalInput").ap()
    d_bandb = nc.dram_tensor("bandb", [128, 128], BF16, kind="ExternalInput").ap()
    d_onesf = nc.dram_tensor("onesf", [128, 128], F32, kind="ExternalInput").ap()

    d_out = nc.dram_tensor("out", [T, D], F32, kind="ExternalOutput").ap()
    scratch_kind = "ExternalOutput"
    d_bufd = nc.dram_tensor("bufd", [BUFROWS, D], BF16, kind=scratch_kind).ap()
    d_yd = nc.dram_tensor("yd", [NS, D], F32, kind=scratch_kind).ap()
    d_atd = nc.dram_tensor("atd", [D, T], BF16, kind=scratch_kind).ap()
    if DEBUG:
        d_rt = nc.dram_tensor("rtd", [128, 256], F32, kind="ExternalOutput").ap()
        d_nrm = nc.dram_tensor("nrm", [2, 2048], F32, kind="ExternalOutput").ap()
        d_xnd = nc.dram_tensor("xnd", [T, D], BF16, kind="ExternalOutput").ap()
        d_qkd = nc.dram_tensor("qkd", [2 * D, T], BF16, kind="ExternalOutput").ap()

    def sb(name, shape, dt):
        return es.enter_context(nc.sbuf_tensor(name, shape, dt))

    ar_x = sb("ar_x", [128, 16, 1024], F32)
    ar_xn = sb("ar_xn", [128, 16, 1024], BF16)
    ar_xnt = sb("ar_xnt", [128, 8, 2048], BF16)
    qt = sb("qt", [128, 8, 2048], BF16)
    kt = sb("kt", [128, 8, 2048], BF16)

    wr_sb = sb("wr_sb", [128, 8, 8], F32)
    wr_bf = sb("wr_bf", [128, 8, 8], BF16)
    b1_sb = sb("b1_sb", [128, E, 32], F32)
    identb = sb("identb_sb", [128, 128], BF16)
    onesb = sb("onesb_sb", [128, 1], BF16)
    trif = sb("trif_sb", [128, 128], F32)
    bandb = sb("bandb_sb", [128, 128], BF16)
    onesf = sb("onesf_sb", [128, 128], F32)

    s1 = sb("s1", [128, 16], F32)
    s2 = sb("s2", [128, 16], F32)
    mean = sb("mean", [128, 16], F32)
    ve = sb("ve", [128, 16], F32)
    srt = sb("srt", [128, 16], F32)
    rstd = sb("rstd", [128, 16], F32)
    negmr = sb("negmr", [128, 16], F32)
    sm = sb("sm", [128, 16], F32)
    rsum = sb("rsum", [128, 16], F32)
    e1 = sb("e1", [128, 16], F32)
    e2 = sb("e2", [128, 16], F32)
    p1 = sb("p1", [128, 16], F32)
    p2 = sb("p2", [128, 16], F32)
    toto = sb("toto", [1, 128], F32)
    offa = sb("offa", [1, 128], F32)
    offb = sb("offb", [1, 128], F32)
    pos1 = sb("pos1", [128, 16], F32)
    pos2 = sb("pos2", [128, 16], F32)
    keep1 = sb("keep1", [128, 16], F32)
    keep2 = sb("keep2", [128, 16], F32)
    pc1 = sb("pc1", [128, 16], F32)
    pc2 = sb("pc2", [128, 16], F32)
    st1 = sb("st1", [128, 16], F32)
    st2 = sb("st2", [128, 16], F32)
    w1c = sb("w1c", [128, 16], F32)
    w2c = sb("w2c", [128, 16], F32)
    sc1 = sb("sc1", [128, 16], I32)
    sc2 = sb("sc2", [128, 16], I32)
    gr1 = sb("gr1", [128, 16], I32)
    gr2 = sb("gr2", [128, 16], I32)
    zero_b = sb("zero_b", [128, 256], BF16)
    negc = sb("negc", [128, 1], F32)
    cns = sb("cns", [1, 16], F32)




    xa = ar_x[:]
    xa_b = ar_x[:].bitcast(BF16)
    av_flat = ar_x[:].rearrange("p a b -> p (a b)").bitcast(BF16)

    def xtile(t):
        return xa[:, t, :]

    def wring(i):
        s = i % 8
        return xa_b[:, s // 2, (s % 2) * 1024:(s % 2) * 1024 + 1024].rearrange(
            "p (k h) -> p k h", k=8)
    wv_bf = ar_x[:, 4:6, :].bitcast(BF16).rearrange("p a b -> p (a b)").rearrange(
        "p (k h) -> p k h", k=8)
    VAUG0 = 6 * 2048
    def w2ring(i):
        s = i % 8
        return xnt_flat[:, (4 + s // 2) * 2048 + (s % 2) * 1024:
                        (4 + s // 2) * 2048 + (s % 2) * 1024 + 1024]
    ybufs = [xa[:, 4, :], xa[:, 5, :]]
    xn_b16 = ar_xn[:]
    sqbuf = [xn_b16[:, 8, :], xn_b16[:, 9, :]]
    g1_sb = xa[:, 4, :]
    g2_sb = xa[:, 5, :]
    xr_sb = [xa[:, 6, :], xa[:, 7, :]]
    boutr_sb = xa[:, 8, :]
    outacc = [xa[:, 13, :], xa[:, 14, :]]
    t1_sb = xa[:, 0, :]
    t2_sb = xa[:, 1, :]
    woutT_res = [xa_b[:, 9 + i, :] for i in range(4)]

    def wout_tile(m):
        return woutT_res[m // 2][:, (m % 2) * 1024:(m % 2) * 1024 + 1024].rearrange(
            "p (k h) -> p k h", k=8)

    def xn_tile(t):
        return ar_xn[:, t, :]

    xn_flat = ar_xn[:].rearrange("p a b -> p (a b)")

    def pt_blk(j):
        return xn_flat[:, j * 512:(j + 1) * 512]

    xnt = ar_xnt[:]
    xnt_flat = ar_xnt[:].rearrange("p a b -> p (a b)")

    def buft(par, k):
        base = par * 3072 + k * 384
        return xnt_flat[:, base:base + 320]

    xnf = ar_xn[:].bitcast(F32)
    relu_sb = xnt_flat[:, 6144:6784].rearrange("p (a b) -> p a b", a=2)
    at_blk = xnt_flat[0:64, 6784:7808].rearrange("p (a b) -> p a b", a=2)
    b2row = ar_xn[0:1, 13:16:2, :]
    bc_sb = ar_xn[0:64, 14, 0:512]
    r_sb = xnf[0:1, 10:12, :]
    r_bf = ar_xn[0:1, 12, 0:1024].rearrange("p (a b) -> p a b", a=2)

    qt_f32 = qt[:].bitcast(F32)
    kt_f32 = kt[:].bitcast(F32)
    ex = kt_f32[:, 6, 0:128]
    m8a = kt_f32[:, 6, 128:256]
    i8a = kt_f32[:, 6, 256:384].bitcast(U32)
    o1 = kt_f32[:, 6, 384:512]
    o2 = kt_f32[:, 6, 512:640]
    oo = kt_f32[:, 6, 640:768]
    posw = kt_f32[:, 6, 768:896]
    pos = kt_f32[:, 6, 896:1024]
    tmpa = kt_f32[:, 7, 0:128]
    tmpb = kt_f32[:, 7, 128:256]

    psum = es.enter_context(nc.psum_tensor("psum", [128, 8, 512], F32))
    psA = psum[:]
    ptr_b = [psA[:, 0, 0:256].bitcast(BF16).rearrange("p (a b) -> p a b", a=4),
             psA[:, 1, 0:256].bitcast(BF16).rearrange("p (a b) -> p a b", a=4)]
    psR = psA[:, 2, 0:128]
    psPW = psA[:, 3, 0:128]
    psTot = psA[0:1, 4, 0:128]
    psOFF = psA[:, 5, 0:128]
    psQK = [psA[:, 0, :], psA[:, 1, :]]
    psV = [psA[:, 2, :], psA[:, 3, :]]
    psNq = psA[0:1, 4:8, :]
    psNk = psA[0:1, 0:4, :]
    psBC1 = psA[:, 6, 0:1]
    psS2 = [psA[:, 0:2, :], psA[:, 2:4, :]]
    psAV2 = [psA[:, 4, :], psA[:, 5, :]]
    psBC2 = [psA[0:64, 6, :], psA[0:64, 7, :]]
    psW1 = [psA[:, 6, 0:320], psA[:, 7, 0:320]]
    psY = [psA[:, b, :] for b in range(6)]
    psO = [psA[:, 6, :], psA[:, 7, :]]

    chans = [
        "LX", "LC", "SQ", "VX", "VVE", "ASQ", "VLN", "AXN", "TTR", "VTE",
        "TRT", "AEX", "VOO", "TCS", "VOF", "TC2", "VRT", "GZ", "GSC",
        "LQ", "VWQ", "TQK", "AQE", "LV", "VWV", "TVV", "AVE",
        "ASQ2", "TNM", "VNM", "ANM", "VNC", "TNC",
        "TSG", "AEXP", "VMSK", "TAV", "VR", "TBC", "VBC", "SAT",
        "VAT", "LBT", "LW1", "LW2", "VW1", "VW2", "TW1", "ARL", "TW2",
        "VYE", "SYD", "LB2", "LWO", "VWO", "LAT", "LXR", "GG", "TPO",
        "VCB", "SOUT", "VZ", "LMISC", "VWR", "VWB2", "VYC",
    ]
    sems = {c: es.enter_context(nc.semaphore("s_" + c)) for c in chans}

    SCHED = []
    for e in range(E):
        SCHED.append(("ffn", e))
        SCHED.append(("attn", 2 * e))
        SCHED.append(("attn", 2 * e + 1))

    def program(W):
        eng = W["eng"]
        h = W["h"]
        C = W["C"]
        marks = {}

        def on(e):
            return eng == e

        def tick(inst, ch, n=1):
            C[ch] = C.get(ch, 0) + n
            if inst is not None:
                inst.then_inc(sems[ch], n)
            return C[ch]

        def cnt(ch):
            return C.get(ch, 0)

        def wait(e, ch, val):
            if on(e) and val > 0:
                h.wait_ge(sems[ch], val)

        def dtick(e, ch, n=1):
            i = h.drain() if on(e) else None
            return tick(i, ch, n)

        for (dst, src) in [(identb[:], d_identb), (trif[:], d_trif),
                           (bandb[:], d_bandb), (onesf[:], d_onesf),
                           (wr_sb[:], d_wr), (b1_sb[:], d_b1t)]:
            i = h.dma_start(out=dst, in_=src) if on("S") else None
            tick(i, "LC", 16)
        for t in range(TT):
            i = h.dma_start(out=xtile(t), in_=d_x[128 * t:128 * (t + 1), :]) if on("S") else None
            tick(i, "LX", 16)

        i = h.memset(zero_b[:], 0.0) if on("V") else None
        if on("V"):
            h.memset(onesb[:], 1.0)
        dtick("V", "VZ", 1)
        wait("V", "LC", 96)
        i = h.tensor_copy(wr_bf[:], wr_sb[:]) if on("V") else None
        dtick("V", "VWR", 1)
        wait("G", "VZ", 1)
        i = None
        if on("G"):
            i = h.dma_start(out=d_bufd.rearrange("(a p) f -> a p f", p=128),
                            in_=zero_b[:].unsqueeze(1).broadcast_to(
                                [128, (BUFROWS * 1024) // (128 * 256), 256]))
        tick(i, "GZ", 16)

        for t in range(TT):
            wait("A", "LX", 16 * (t + 1))
            wait("A", "VX", max(0, t - 1))
            i = None
            if on("A"):
                i = h.activation(out=qt_f32[:, t % 2, :], in_=xtile(t), func=AF.Square)
            dtick("A", "SQ", 1)
            wait("V", "LX", 16 * (t + 1))
            if on("V"):
                h.reduce_sum(s1[:, t:t + 1], xtile(t), axis=AX.X)
            wait("V", "SQ", t + 1)
            i = None
            if on("V"):
                i = h.reduce_sum(s2[:, t:t + 1], qt_f32[:, t % 2, :], axis=AX.X)
            dtick("V", "VX", 1)
        i = None
        if on("V"):
            h.drain()
            h.tensor_scalar(mean[:], s1[:], 1.0 / D, None, OP.mult)
            h.drain()
            h.tensor_tensor(tmpa[:, 0:16], mean[:], mean[:], OP.mult)
            h.drain()
            h.scalar_tensor_tensor(ve[:], s2[:], 1.0 / D, tmpa[:, 0:16],
                                    OP.mult, OP.subtract)
            h.drain()
            i = h.tensor_scalar(ve[:], ve[:], 1e-5, None, OP.add)
        dtick("V", "VVE", 1)
        wait("A", "VVE", 1)
        i = None
        if on("A"):
            h.drain()
            i = h.activation(out=srt[:], in_=ve[:], func=AF.Sqrt)
        dtick("A", "ASQ", 1)
        wait("V", "ASQ", 1)
        i = None
        if on("V"):
            h.reciprocal(rstd[:], srt[:])
            h.drain()
            h.tensor_tensor(tmpa[:, 16:32], mean[:], rstd[:], OP.mult)
            h.drain()
            i = h.tensor_scalar(negmr[:], tmpa[:, 16:32], -1.0, None, OP.mult)
        dtick("V", "VLN", 1)
        for t in range(TT):
            wait("A", "VLN", 1)
            i = None
            if on("A"):
                i = h.activation(out=xn_tile(t), in_=xtile(t), func=AF.Identity,
                                 bias=negmr[:, t:t + 1], scale=rstd[:, t:t + 1])
            dtick("A", "AXN", 1)

        for t in range(TT):
            for g in range(2):
                gi = 2 * t + g
                wait("T", "AXN", t + 1)
                wait("T", "LC", 16)
                wait("T", "VTE", max(0, gi - 1))
                i = None
                if on("T"):
                    for c4 in range(4):
                        c = 4 * g + c4
                        h.transpose(ptr_b[gi % 2][:, c4, :],
                                    xn_tile(t)[:, 128 * c:128 * (c + 1)], identb[:])
                    i = h.drain()
                tick(i, "TTR", 1)
                wait("V", "TTR", gi + 1)
                i = None
                if on("V"):
                    i = h.tensor_copy(xnt[:, 4 * g:4 * g + 4, 128 * t:128 * (t + 1)],
                                      ptr_b[gi % 2][:])
                dtick("V", "VTE", 1)

        wait("T", "VTE", 32)
        wait("T", "LC", 96)
        wait("T", "VWR", 1)
        i = None
        if on("T"):
            for t in range(TT):
                for k in range(CT):
                    i = h.matmul(psR[:, 8 * t:8 * t + 8],
                                 xnt[:, k, 128 * t:128 * (t + 1)], wr_bf[:, k, :],
                                 start=(k == 0), stop=(k == CT - 1))
        tick(i, "TRT", 1)
        wait("A", "TRT", 1)
        i = None
        if on("A"):
            i = h.activation(out=ex, in_=psR, func=AF.Exp)
        dtick("A", "AEX", 1)
        wait("V", "AEX", 1)
        i = None
        if on("V"):
            exv = ex.rearrange("p (g e) -> p g e", e=8)
            h.reduce_sum(sm[:], exv, axis=AX.X)
            h.drain()
            h.reciprocal(rsum[:], sm[:])
            for t in range(TT):
                h.max(m8a[:, 8 * t:8 * t + 8], ex[:, 8 * t:8 * t + 8])
            h.drain()
            for t in range(TT):
                h.max_index(i8a[:, 8 * t:8 * t + 8], m8a[:, 8 * t:8 * t + 8],
                            ex[:, 8 * t:8 * t + 8])
            h.drain()
            m8v = m8a.rearrange("p (g e) -> p g e", e=8)
            i8v = i8a.rearrange("p (g e) -> p g e", e=8)
            h.tensor_copy(e1[:], i8v[:, :, 0])
            h.tensor_copy(e2[:], i8v[:, :, 1])
            h.tensor_tensor(p1[:], m8v[:, :, 0], rsum[:], OP.mult)
            h.tensor_tensor(p2[:], m8v[:, :, 1], rsum[:], OP.mult)
            h.drain()
            o1v = o1.rearrange("p (g e) -> p g e", e=8)
            o2v = o2.rearrange("p (g e) -> p g e", e=8)
            for j in range(8):
                h.tensor_scalar(o1v[:, :, j], e1[:], float(j), None, OP.is_equal)
                h.tensor_scalar(o2v[:, :, j], e2[:], float(j), None, OP.is_equal)
            h.drain()
            i = h.tensor_tensor(oo, o1, o2, OP.add)
        dtick("V", "VOO", 1)
        wait("T", "VOO", 1)
        i = None
        if on("T"):
            for t in range(TT):
                i = h.matmul(psPW[:, 8 * t:8 * t + 8], trif[:],
                             oo[:, 8 * t:8 * t + 8], start=True, stop=True)
                i = h.matmul(psTot[:, 8 * t:8 * t + 8], onesf[:, 0:1],
                             oo[:, 8 * t:8 * t + 8], start=True, stop=True)
        tick(i, "TCS", 1)
        wait("V", "TCS", 1)
        i = None
        if on("V"):
            h.tensor_copy(toto[:], psTot)
            h.drain()
            h.memset(offa[:, 0:8], 0.0)
            h.tensor_copy(offa[:, 8:128], toto[:, 0:120])
            h.drain()
            bufs = [offa, offb]
            cur = 0
            for s in (8, 16, 32, 64):
                src, dst = bufs[cur], bufs[1 - cur]
                h.tensor_copy(dst[:, 0:s], src[:, 0:s])
                h.tensor_tensor(dst[:, s:128], src[:, s:128], src[:, 0:128 - s], OP.add)
                h.drain()
                cur = 1 - cur
            i = h.engine_nop()
        dtick("V", "VOF", 1)
        wait("T", "VOF", 1)
        i = None
        if on("T"):
            i = h.matmul(psOFF, onesf[0:1, :], offa[:], start=True, stop=True)
        tick(i, "TC2", 1)
        wait("V", "TC2", 1)
        i = None
        if on("V"):
            h.tensor_copy(posw, psPW)
            h.drain()
            h.tensor_tensor(pos, posw, psOFF, OP.add)
            h.drain()
            h.tensor_tensor(tmpa, pos, o1, OP.mult)
            h.tensor_tensor(tmpb, pos, o2, OP.mult)
            h.drain()
            h.reduce_sum(pos1[:], tmpa.rearrange("p (g e) -> p g e", e=8), axis=AX.X)
            h.reduce_sum(pos2[:], tmpb.rearrange("p (g e) -> p g e", e=8), axis=AX.X)
            h.drain()
            h.tensor_scalar(keep1[:], pos1[:], float(CAP), None, OP.is_lt)
            h.tensor_scalar(keep2[:], pos2[:], float(CAP), None, OP.is_lt)
            h.tensor_scalar(pc1[:], pos1[:], float(CAP - 1), None, OP.min)
            h.tensor_scalar(pc2[:], pos2[:], float(CAP - 1), None, OP.min)
            h.drain()
            h.tensor_scalar(tmpa[:, 0:16], e1[:], float(CAP), None, OP.mult)
            h.tensor_scalar(tmpa[:, 16:32], e2[:], float(CAP), None, OP.mult)
            h.tensor_tensor(w1c[:], p1[:], keep1[:], OP.mult)
            h.tensor_tensor(w2c[:], p2[:], keep2[:], OP.mult)
            h.drain()
            h.tensor_tensor(st1[:], tmpa[:, 0:16], pc1[:], OP.add)
            h.tensor_tensor(st2[:], tmpa[:, 16:32], pc2[:], OP.add)
            h.drain()
            h.tensor_copy(gr1[:], st1[:])
            h.tensor_copy(gr2[:], st2[:])
            h.tensor_scalar(tmpa[:, 32:48], st1[:], -1.0, float(NS), OP.mult, OP.add)
            h.tensor_scalar(tmpa[:, 48:64], st2[:], -1.0, float(NS), OP.mult, OP.add)
            h.tensor_scalar(tmpa[:, 64:80], keep1[:], -1.0, 1.0, OP.mult, OP.add)
            h.tensor_scalar(tmpa[:, 80:96], keep2[:], -1.0, 1.0, OP.mult, OP.add)
            h.drain()
            h.tensor_tensor(tmpb[:, 0:16], tmpa[:, 32:48], tmpa[:, 64:80], OP.mult)
            h.tensor_tensor(tmpb[:, 16:32], tmpa[:, 48:64], tmpa[:, 80:96], OP.mult)
            h.drain()
            h.tensor_tensor(tmpa[:, 96:112], st1[:], tmpb[:, 0:16], OP.add)
            h.tensor_tensor(tmpa[:, 112:128], st2[:], tmpb[:, 16:32], OP.add)
            h.drain()
            h.tensor_copy(sc1[:], tmpa[:, 96:112])
            h.tensor_copy(sc2[:], tmpa[:, 112:128])
            h.drain()
            i = h.engine_nop()
        dtick("V", "VRT", 1)

        wait("G", "VRT", 1)
        wait("G", "AXN", TT)
        wait("G", "GZ", 16)
        for t in range(TT):
            for sc in (sc1, sc2):
                i = None
                if on("G"):
                    i = h.indirect_dma_start(
                        out=d_bufd,
                        out_offset=IndirectOffsetOnAxis(sc[:, t:t + 1], axis=0),
                        in_=xn_tile(t), in_offset=None)
                tick(i, "GSC", 16)
        GSC_ALL = cnt("GSC")

        for m in range(16):
            wait("G", "TQK", marks.get(("tqk", m - 8), 0))
            if m < 8:
                wait("G", "VX", TT)
                wait("G", "AXN", TT)
            i = None
            if on("G"):
                i = h.dma_start(out=wring(m), in_=d_wqkvp[m])
            tick(i, "LQ", 16)
            for ch in range(QC):
                q4 = QC * m + ch
                wait("T", "LQ", 16 * (m + 1))
                wait("T", "AQE", max(0, q4 - 1))
                i = None
                if on("T"):
                    for k in range(CT):
                        i = h.matmul(psQK[q4 % 2], wring(m)[:, k, :],
                                     xnt[:, k, 512 * ch:512 * (ch + 1)],
                                     start=(k == 0), stop=(k == CT - 1))
                tick(i, "TQK", 1)
                if ch == QC - 1:
                    marks[("tqk", m)] = cnt("TQK")
                wait("A", "TQK", q4 + 1)
                if m >= 14 and ch == 0:
                    wait("A", "VRT", 1)
                i = None
                if on("A"):
                    dst = qt if m < 8 else kt
                    i = h.activation(out=dst[:, m % 8, 512 * ch:512 * (ch + 1)],
                                     in_=psQK[q4 % 2], func=AF.Copy)
                dtick("A", "AQE", 1)
        for vc in range(2):
            wait("G", "TVV", 16 * vc)
            wait("G", "VX", TT)
            wait("G", "AXN", TT)
            i = None
            if on("G"):
                i = h.dma_start(out=wv_bf, in_=d_wvp[vc])
            tick(i, "LV", 16)
            for tt_ in range(TT):
                vidx = 16 * vc + tt_
                wait("T", "LV", 16 * (vc + 1))
                wait("T", "AVE", max(0, vidx - 1))
                if vidx == 0:
                    wait("T", "VRT", 1)
                    wait("T", "AEX", 1)
                i = None
                if on("T"):
                    for k in range(CT):
                        i = h.matmul(psV[vidx % 2], xnt[:, k, 128 * tt_:128 * (tt_ + 1)],
                                     wv_bf[:, k, :], start=(k == 0), stop=(k == CT - 1))
                tick(i, "TVV", 1)
                wait("A", "TVV", vidx + 1)
                i = None
                if on("A"):
                    base = VAUG0 + tt_ * 1040 + (8 * vc) * 65
                    outap = av_flat[:, base:base + 8 * 65].rearrange(
                        "p (hd c) -> p hd c", hd=8)[:, :, 0:64]
                    i = h.activation(out=outap, in_=psV[vidx % 2].rearrange(
                        "p (hd c) -> p hd c", hd=8), func=AF.Copy)
                dtick("A", "AVE", 1)
        wait("V", "AVE", 32)
        i = None
        if on("V"):
            for jt in range(TT):
                base = VAUG0 + jt * 1040 + 64
                i = h.memset(av_flat[:, base:base + 16 * 65].rearrange(
                    "p (hd c) -> p hd c", c=65)[:, :, 0:1], 1.0)
        dtick("V", "VAT", 1)

        for i_ in range(32):
            src = qt if i_ < 16 else kt
            ct_, hf = (i_ % 16) // 2, i_ % 2
            if i_ == 0:
                wait("A", "GSC", GSC_ALL)
            wait("A", "TNM", max(0, i_ - 1))
            i = None
            if on("A"):
                if i_ == 0:
                    h.drain()
                i = h.activation(out=sqbuf[i_ % 2],
                                 in_=src[:, ct_, 1024 * hf:1024 * (hf + 1)],
                                 func=AF.Square)
            dtick("A", "ASQ2", 1)
            wait("T", "ASQ2", i_ + 1)
            if i_ == 0:
                wait("T", "VRT", 1)
            if i_ == 16:
                wait("T", "AVE", 32)
            i = None
            if on("T"):
                ps = psNq if i_ < 16 else psNk
                for half in range(2):
                    i = h.matmul(ps[:, 2 * hf + half, :], onesb[:],
                                 sqbuf[i_ % 2][:, 512 * half:512 * (half + 1)],
                                 start=(ct_ == 0), stop=(ct_ == 7))
            tick(i, "TNM", 1)
        wait("V", "TNM", 32)
        i = None
        if on("V"):
            h.reduce_max(cns[:, 0:1], psNq.rearrange("p a b -> p (a b)"), axis=AX.X)
            h.reduce_max(cns[:, 1:2], psNk.rearrange("p a b -> p (a b)"), axis=AX.X)
            if DEBUG:
                h.tensor_copy(r_sb[:, 0, :], psNq[:, 0, :])
            h.drain()
            i = h.engine_nop()
        dtick("V", "VNM", 1)
        if DEBUG:
            wait("S", "VNM", cnt("VNM"))
            i = None
            if on("S"):
                i = h.dma_start(out=d_nrm[0:1, 0:512], in_=r_sb[:, 0, :])
            tick(i, "LMISC", 16)
        wait("A", "VNM", 1)
        i = None
        if on("A"):
            h.drain()
            i = h.activation(out=cns[:, 2:4], in_=cns[:, 0:2], func=AF.Sqrt)
        dtick("A", "ANM", 1)
        wait("V", "ANM", 1)
        i = None
        if on("V"):
            h.drain()
            h.tensor_tensor(cns[:, 4:5], cns[:, 2:3], cns[:, 3:4], OP.mult)
            h.drain()
            h.tensor_scalar(cns[:, 5:6], cns[:, 4:5], 0.1275, None, OP.mult)
            h.drain()
            h.tensor_scalar(cns[:, 6:7], cns[:, 5:6], -75.0, 0.0, OP.add, OP.max)
            h.drain()
            i = h.tensor_scalar(cns[:, 7:8], cns[:, 6:7], -1.0, None, OP.mult)
        dtick("V", "VNC", 1)
        wait("T", "VNC", 1)
        i = None
        if on("T"):
            i = h.matmul(psBC1, onesf[0:1, :], cns[0:1, 7:8], start=True, stop=True)
        tick(i, "TNC", 1)
        wait("V", "TNC", 1)
        i = None
        if on("V"):
            i = h.tensor_copy(negc[:], psBC1)
        dtick("V", "VNC", 1)

        if DEBUG:
            wait("S", "AQE", 64)
            wait("S", "AVE", 32)
            for ct_ in range(8):
                i = None
                if on("S"):
                    i = h.dma_start(out=d_qkd[128 * ct_:128 * (ct_ + 1), :],
                                    in_=qt[:, ct_, :])
                tick(i, "LMISC", 16)
                i = None
                if on("S"):
                    i = h.dma_start(out=d_qkd[1024 + 128 * ct_:1024 + 128 * (ct_ + 1), :],
                                    in_=kt[:, ct_, :])
                tick(i, "LMISC", 16)

        wait("S", "AQE", 64)
        wait("S", "AVE", 32)
        at_cnt = 0
        first_exp = True
        for kind, idx in SCHED:
            if kind == "attn":
                hh = idx
                ct2, pb = hh // 2, 64 * (hh % 2)
                for q in range(QC):
                    cidx = at_cnt
                    jmax = 4 * (q + 1)
                    ngrp = 2 * (q + 1)
                    for g in range(ngrp):
                        gidx_a = C.get("_agrp", 0)
                        C["_agrp"] = gidx_a + 1
                        wait("T", "AEXP", marks.get(("exp", gidx_a - 2), 0))
                        if q == 0 and g == 0 and hh % 2 == 0:
                            wait("T", "VYC", cnt("VYC"))
                        if hh == 0 and q == 0 and g == 0:
                            wait("T", "AQE", 64)
                            wait("T", "AVE", 32)
                            wait("T", "VNC", 2)
                            wait("T", "VNM", 1)
                        i = None
                        if on("T"):
                            for j2 in range(2):
                                j = 2 * g + j2
                                i = h.matmul(psS2[g % 2][:, j2, :],
                                             kt[pb:pb + 64, ct2, 128 * j:128 * (j + 1)],
                                             qt[pb:pb + 64, ct2, 512 * q:512 * (q + 1)],
                                             start=True, stop=True)
                        tick(i, "TSG", 1)
                        wait("A", "TSG", cnt("TSG"))
                        wait("A", "TAV", marks.get(("avdone", cidx - 1), 0))
                        if first_exp:
                            wait("A", "GSC", GSC_ALL)
                            first_exp = False
                        i = None
                        if on("A"):
                            i = h.activation(
                                out=xn_flat[:, 2 * g * 512:(2 * g + 2) * 512],
                                in_=psS2[g % 2].rearrange("p a b -> p (a b)"),
                                func=AF.Exp, bias=negc[:, 0:1], scale=0.125)
                        dtick("A", "AEXP", 1)
                        marks[("exp", gidx_a)] = cnt("AEXP")
                    wait("V", "AEXP", cnt("AEXP"))
                    i = None
                    if on("V"):
                        for m_ in range(4):
                            j = 4 * q + m_
                            if m_ > 0:
                                h.memset(pt_blk(j)[:, 0:128 * m_], 0.0)
                            h.tensor_tensor(pt_blk(j)[:, 128 * m_:128 * (m_ + 1)],
                                            pt_blk(j)[:, 128 * m_:128 * (m_ + 1)],
                                            bandb[:], OP.mult)
                        i = h.engine_nop()
                    dtick("V", "VMSK", 1)
                    wait("T", "VMSK", cnt("VMSK"))
                    wait("T", "VAT", 1)
                    wait("T", "VBC", marks.get(("vbc", cidx - 2), 0))
                    i = None
                    if on("T"):
                        for j in range(jmax):
                            base = VAUG0 + j * 1040 + hh * 65
                            i = h.matmul(psAV2[cidx % 2][0:65, :],
                                         av_flat[:, base:base + 65],
                                         pt_blk(j), start=(j == 0), stop=(j == jmax - 1))
                    tick(i, "TAV", 1)
                    marks[("avdone", cidx)] = cnt("TAV")
                    wait("A", "TAV", cnt("TAV"))
                    i = None
                    if on("A"):
                        h.drain()
                        h.activation(out=r_sb[:, cidx % 2, :],
                                     in_=psAV2[cidx % 2][64:65, :], func=AF.Ln)
                        h.drain()
                        i = h.activation(out=r_bf[:, cidx % 2, :],
                                         in_=r_sb[:, cidx % 2, :], func=AF.Exp,
                                         scale=-1.0)
                    dtick("A", "VR", 1)
                    wait("T", "VR", cnt("VR"))
                    i = None
                    if on("T"):
                        i = h.matmul(psBC2[cidx % 2], bandb[0:1, 0:64],
                                     r_bf[:, cidx % 2, :], start=True, stop=True)
                    tick(i, "TBC", 1)
                    wait("V", "TBC", cnt("TBC"))
                    if at_cnt == 0:
                        wait("V", "TVV", 32)
                    wait("V", "SAT", 16 * max(0, at_cnt - 1))
                    i = None
                    if on("V"):
                        h.drain()
                        h.tensor_copy(bc_sb[:], psBC2[cidx % 2])
                        h.drain()
                        i = h.tensor_tensor(at_blk[:, at_cnt % 2, :],
                                            psAV2[cidx % 2][0:64, :],
                                            bc_sb[:], OP.mult)
                    dtick("V", "VBC", 1)
                    marks[("vbc", cidx)] = cnt("VBC")
                    wait("S", "VBC", cnt("VBC"))
                    i = None
                    if on("S"):
                        i = h.dma_start(
                            out=d_atd[64 * hh:64 * (hh + 1), 512 * q:512 * (q + 1)],
                            in_=at_blk[:, at_cnt % 2, :])
                    tick(i, "SAT", 16)
                    at_cnt += 1
            else:
                e = idx
                wait("S", "GSC", GSC_ALL)
                if e < 2:
                    wait("S", "TVV", 32)
                    wait("S", "TRT", 1)
                    wait("S", "TQK", 64)
                else:
                    wait("S", "TW1", 32 * (e - 1))
                for c in range(CT):
                    i = None
                    if on("S"):
                        i = h.dma_start(out=buft(e % 2, c),
                                        in_=d_bufd[CAP * e:CAP * (e + 1),
                                                   128 * c:128 * (c + 1)],
                                        transpose=True)
                    tick(i, "LBT", 16)
                wait("G", "TW2", marks.get(("w2b2", e - 2), 0))
                if e == 0:
                    wait("G", "TVV", 32)
                    wait("G", "TQK", 64)
                i = None
                if on("G"):
                    i = h.dma_start(out=b2row[:, e % 2, :], in_=d_b2r[e, 0:1, :])
                tick(i, "VWB2", 16)
                for ht in range(32):
                    gidx = 32 * e + ht
                    wait("G", "TW1", marks.get(("w1", gidx - 8), 0))
                    if gidx < 8:
                        wait("G", "TQK", 64)
                        wait("G", "TNC", 1)
                    i = None
                    if on("G"):
                        i = h.dma_start(out=wring(gidx), in_=d_w1p[e, ht])
                    tick(i, "LW1", 16)
                    wait("G", "TW2", marks.get(("w2", gidx - 8), 0))
                    if gidx < 8:
                        wait("G", "TVV", 32)
                    i = None
                    if on("G"):
                        i = h.dma_start(out=w2ring(gidx),
                                        in_=d_w2[e, 128 * ht:128 * (ht + 1), :])
                    tick(i, "LW2", 16)
                    wait("T", "LW1", 16 * (gidx + 1))
                    wait("T", "LBT", 16 * 8 * (e + 1))
                    wait("T", "ARL", max(0, cnt("ARL") - 1))
                    if gidx == 0:
                        wait("T", "VNC", 2)
                    if ht == 0:
                        wait("T", "VBC", cnt("VBC"))
                    i = None
                    if on("T"):
                        for k in range(CT):
                            i = h.matmul(psW1[ht % 2], wring(gidx)[:, k, :],
                                         buft(e % 2, k), start=(k == 0),
                                         stop=(k == CT - 1))
                    tick(i, "TW1", 1)
                    marks[("w1", gidx)] = cnt("TW1")
                    wait("A", "TW1", cnt("TW1"))
                    wait("A", "TW2", marks.get(("w2", gidx - 2), 0))
                    i = None
                    if on("A"):
                        i = h.activation(out=relu_sb[:, ht % 2, :], in_=psW1[ht % 2],
                                         func=AF.Relu, bias=b1_sb[:, e, ht:ht + 1])
                    dtick("A", "ARL", 1)
                    marks[("arl", gidx)] = cnt("ARL")
                    if ht > 0:
                        pht = ht - 1
                        wait("T", "ARL", marks.get(("arl", gidx - 1), 0))
                        wait("T", "LW2", 16 * gidx)
                        if pht == 0:
                            wait("T", "VYE", marks.get(("vye", e - 1), 0))
                            wait("T", "AEXP", cnt("AEXP"))
                        i = None
                        if on("T"):
                            for sc_ in range(3):
                                rows = 64 if sc_ == 2 else 128
                                for cc in range(2):
                                    i = h.matmul(psY[2 * sc_ + cc][0:rows, :],
                                                 relu_sb[:, pht % 2, 128 * sc_:128 * sc_ + rows],
                                                 w2ring(gidx - 1)[:, 512 * cc:512 * (cc + 1)],
                                                 start=(pht == 0), stop=False)
                        tick(i, "TW2", 6)
                        marks[("w2", gidx - 1)] = cnt("TW2")
                    if ht == 31:
                        wait("T", "ARL", marks.get(("arl", gidx), 0))
                        wait("T", "LW2", 16 * (gidx + 1))
                        wait("T", "VWB2", cnt("VWB2"))
                        i = None
                        if on("T"):
                            for sc_ in range(3):
                                rows = 64 if sc_ == 2 else 128
                                for cc in range(2):
                                    h.matmul(psY[2 * sc_ + cc][0:rows, :],
                                             relu_sb[:, ht % 2, 128 * sc_:128 * sc_ + rows],
                                             w2ring(gidx)[:, 512 * cc:512 * (cc + 1)],
                                             start=False, stop=False)
                            for sc_ in range(3):
                                rows = 64 if sc_ == 2 else 128
                                for cc in range(2):
                                    i = h.matmul(psY[2 * sc_ + cc][0:rows, :],
                                                 bandb[0:1, 0:rows],
                                                 b2row[:, e % 2, 512 * cc:512 * (cc + 1)],
                                                 start=False, stop=True)
                        tick(i, "TW2", 6)
                        marks[("w2", gidx)] = cnt("TW2")
                        marks[("w2b2", e)] = cnt("TW2")
                for sc_ in range(3):
                    rows = 64 if sc_ == 2 else 128
                    yi = 3 * e + sc_
                    wait("V", "TW2", cnt("TW2"))
                    wait("V", "SYD", marks.get(("syd", yi - 2), 0))
                    i = None
                    if on("V"):
                        i = h.tensor_copy(ybufs[yi % 2][0:rows, :],
                                          psY[2 * sc_:2 * sc_ + 2]
                                          if False else
                                          psA[0:rows, 2 * sc_:2 * sc_ + 2, :]
                                          .rearrange("p a b -> p (a b)"))
                    dtick("V", "VYC", 1)
                    marks[("vyc", yi)] = cnt("VYC")
                    wait("S", "VYC", marks[("vyc", yi)])
                    i = None
                    if on("S"):
                        i = h.dma_start(
                            out=d_yd[CAP * e + 128 * sc_:CAP * e + 128 * sc_ + rows, :],
                            in_=ybufs[yi % 2][0:rows, :])
                    tick(i, "SYD", 16)
                    marks[("syd", yi)] = cnt("SYD")
                i = h.engine_nop() if on("V") else None
                dtick("V", "VYE", 1)
                marks[("vye", e)] = cnt("VYE")
        SYD_ALL = cnt("SYD")
        SAT_ALL = cnt("SAT")

        for m in range(8):
            if m == 0:
                wait("G", "SYD", SYD_ALL)
                wait("G", "TAV", 64)
            i = None
            if on("G"):
                i = h.dma_start(out=wout_tile(m), in_=d_woutp[m])
            tick(i, "LWO", 16)
        i = h.dma_start(out=boutr_sb, in_=d_boutr) if on("S") else None
        tick(i, "LMISC", 16)

        TSG_ALL = cnt("TSG")
        for t in range(TT):
            if t == 0:
                wait("S", "TSG", TSG_ALL)
            wait("S", "TPO", 2 * max(0, t - 1))
            i = None
            if on("S"):
                i = h.dma_start(out=kt[:, 0:8, 128 * (t % 2):128 * (t % 2) + 128],
                                in_=d_atd[:, 128 * t:128 * (t + 1)].rearrange(
                                    "(k p) c -> p k c", p=128))
            tick(i, "LAT", 16)
            wait("S", "VCB", max(0, cnt("VCB") - 1))
            i = None
            if on("S"):
                i = h.dma_start(out=xr_sb[t % 2], in_=d_x[128 * t:128 * (t + 1), :])
            tick(i, "LXR", 16)
            wait("G", "SYD", SYD_ALL)
            wait("G", "VCB", cnt("VCB"))
            i = None
            if on("G"):
                i = h.indirect_dma_start(out=g1_sb, out_offset=None, in_=d_yd,
                                         in_offset=IndirectOffsetOnAxis(
                                             gr1[:, t:t + 1], axis=0))
            tick(i, "GG", 16)
            i = None
            if on("G"):
                i = h.indirect_dma_start(out=g2_sb, out_offset=None, in_=d_yd,
                                         in_offset=IndirectOffsetOnAxis(
                                             gr2[:, t:t + 1], axis=0))
            tick(i, "GG", 16)
            wait("T", "LAT", cnt("LAT"))
            wait("T", "LWO", 16 * 8)
            wait("T", "VCB", cnt("VCB"))
            if t == 0:
                wait("T", "ARL", 256)
                wait("T", "VBC", cnt("VBC"))
            i = None
            if on("T"):
                for m in range(8):
                    for k in range(CT):
                        i = h.matmul(psO[m // 4][:, 128 * (m % 4):128 * (m % 4) + 128],
                                     kt[:, k, 128 * (t % 2):128 * (t % 2) + 128],
                                     wout_tile(m)[:, k, :],
                                     start=(k == 0), stop=(k == CT - 1))
            tick(i, "TPO", 2)
            wait("V", "TPO", cnt("TPO"))
            wait("V", "GG", cnt("GG"))
            wait("V", "LXR", 16 * (t + 1))
            wait("V", "LMISC", 16)
            wait("V", "SOUT", 16 * max(0, t - 1))
            i = None
            if on("V"):
                h.drain()
                h.tensor_scalar(t1_sb, g1_sb, w1c[:, t:t + 1], None, OP.mult)
                h.drain()
                h.scalar_tensor_tensor(t2_sb, g2_sb, w2c[:, t:t + 1], t1_sb,
                                       OP.mult, OP.add)
                h.drain()
                for cc in range(2):
                    h.tensor_tensor(t1_sb[:, 512 * cc:512 * (cc + 1)], psO[cc],
                                    t2_sb[:, 512 * cc:512 * (cc + 1)], OP.add)
                h.drain()
                h.tensor_tensor(t2_sb, xr_sb[t % 2], boutr_sb, OP.add)
                h.drain()
                i = h.tensor_tensor(outacc[t % 2], t1_sb, t2_sb, OP.add)
            dtick("V", "VCB", 1)
            wait("S", "VCB", cnt("VCB"))
            i = None
            if on("S"):
                i = h.dma_start(out=d_out[128 * t:128 * (t + 1), :], in_=outacc[t % 2])
            tick(i, "SOUT", 16)
        if DEBUG:
            wait("V", "VCB", cnt("VCB"))
            i = None
            if on("V"):
                h.drain()
                rtd_sb = outacc[1]
                h.tensor_copy(rtd_sb[:, 0:16], p1[:])
                h.tensor_copy(rtd_sb[:, 16:32], e1[:])
                h.tensor_copy(rtd_sb[:, 32:48], pos1[:])
                h.tensor_copy(rtd_sb[:, 48:64], w1c[:])
                h.tensor_copy(rtd_sb[:, 64:80], keep1[:])
                h.tensor_copy(rtd_sb[:, 80:96], st1[:])
                h.tensor_copy(rtd_sb[:, 96:112], gr1[:])
                h.tensor_copy(rtd_sb[:, 112:128], p2[:])
                h.tensor_copy(rtd_sb[:, 128:144], e2[:])
                h.tensor_copy(rtd_sb[:, 144:160], pos2[:])
                h.tensor_copy(rtd_sb[:, 160:176], w2c[:])
                h.tensor_copy(rtd_sb[:, 176:192], gr2[:])
                h.tensor_copy(rtd_sb[0:1, 192:208], cns[:])
                h.tensor_copy(rtd_sb[:, 208:209], negc[:])
                h.drain()
                i = h.engine_nop()
            dtick("V", "VCB", 1)
            wait("S", "VCB", cnt("VCB"))
            i = None
            if on("S"):
                i = h.dma_start(out=d_rt[:, 0:256], in_=outacc[1][:, 0:256])
            tick(i, "SOUT", 16)
        if on("G"):
            h.wait_ge(sems["SOUT"], cnt("SOUT"))

    block = es.enter_context(nc.Block())

    def mk(eng):
        def f(hh):
            program({"eng": eng, "h": hh, "C": {}})
        return f

    block.sync(mk("S"))
    block.gpsimd(mk("G"))
    block.vector(mk("V"))
    block.scalar(mk("A"))
    block.tensor(mk("T"))
    es.close()
    return nc


_NC_CACHE = {}


def _get_nc():
    if "nc" not in _NC_CACHE:
        _NC_CACHE["nc"] = build_core()
    return _NC_CACHE["nc"]


def kernel(L0_attn_x, L0_attn_Wqkv, L0_attn_Wout, L0_attn_bout,
           L0_ffn_Wr, L0_ffn_W1, L0_ffn_b1, L0_ffn_W2, L0_ffn_b2):
    x = np.asarray(L0_attn_x, dtype=np.float32)
    wqkv = np.asarray(L0_attn_Wqkv, dtype=np.float32)
    wout = np.asarray(L0_attn_Wout, dtype=np.float32)
    bout = np.asarray(L0_attn_bout, dtype=np.float32)
    wr = np.asarray(L0_ffn_Wr, dtype=np.float32)
    w1 = np.ascontiguousarray(np.asarray(L0_ffn_W1, dtype=np.float32))
    b1 = np.asarray(L0_ffn_b1, dtype=np.float32)
    w2 = np.ascontiguousarray(np.asarray(L0_ffn_W2, dtype=np.float32))
    b2 = np.asarray(L0_ffn_b2, dtype=np.float32)

    wqkvp = np.ascontiguousarray(
        wqkv[:2048].reshape(16, 128, 8, 128).transpose(0, 3, 2, 1))
    wvp = np.ascontiguousarray(
        wqkv[2048:3072].reshape(2, 512, 8, 128).transpose(0, 3, 2, 1))
    woutp = np.ascontiguousarray(
        wout.reshape(8, 128, 8, 128).transpose(0, 3, 2, 1))
    w1p = np.ascontiguousarray(
        w1.reshape(E, 8, 128, 32, 128).transpose(0, 3, 2, 1, 4))
    boutrep = np.ascontiguousarray(np.broadcast_to(bout[None, :], (128, D)))
    wrp = np.ascontiguousarray(wr.reshape(8, 128, 8).transpose(1, 0, 2))
    b1t = np.ascontiguousarray(b1.reshape(E, 32, 128).transpose(2, 0, 1))
    b2rep = np.ascontiguousarray(np.broadcast_to(b2[:, None, :], (E, 128, D)))
    identb = np.eye(128, dtype=ml_dtypes.bfloat16)
    trif = np.triu(np.ones((128, 128), np.float32), 1)
    bandb = np.triu(np.ones((128, 128), np.float32), 0).astype(ml_dtypes.bfloat16)
    onesf = np.ones((128, 128), np.float32)

    shared = dict(wqkvp=wqkvp, wvp=wvp, woutp=woutp, boutrep=boutrep, wrp=wrp,
                  w1p=w1p, w2=w2, b1t=b1t, b2rep=b2rep, identb=identb,
                  trif=trif, bandb=bandb, onesf=onesf)
    in_maps = [dict(shared, x=np.ascontiguousarray(x[b])) for b in range(8)]

    nc = _get_nc()
    res = run_bass_kernel_spmd(nc, in_maps, core_ids=list(range(8)))
    out = np.stack([r["out"] for r in res.results], axis=0)
    return out
